# version 21
# baseline (speedup 1.0000x reference)
import numpy as np
import ml_dtypes
import concourse.bass as bass
import concourse.mybir as mybir
from concourse.bass_utils import run_bass_kernel_spmd

F32 = mybir.dt.float32
BF16 = mybir.dt.bfloat16
Alu = mybir.AluOpType
Act = mybir.ActivationFunctionType
AX = mybir.AxisListType

D = 1024
L = 4096
LQ = 2048
H = 16
FI = 2816
ICH = 256
NIC = FI // ICH
VPW = 132


class Buf:
    __slots__ = ("ready", "users")
    def __init__(self):
        self.ready = []
        self.users = []


class Prog:
    ENGINES = ("sync", "gpsimd", "vector", "scalar", "tensor")
    DMA_ENG = {"sync", "gpsimd"}

    def __init__(self, nc):
        self.nc = nc
        self.ops = []
        self.count = {e: 0 for e in self.ENGINES}
        self.B = {}

    def buf(self, key):
        if key not in self.B:
            self.B[key] = Buf()
        return self.B[key]

    def alias_seed(self, new_keys, old_keys):
        toks = []
        for ok in old_keys:
            b = self.buf(ok)
            toks += b.ready + b.users
        for nk in new_keys:
            self.buf(nk).users.extend(toks)

    def _sem_key(self, eng, meth, args):
        if meth != "dma_start":
            return eng, 1
        for ap in args[:2]:
            try:
                if str(ap.space).endswith("SBUF"):
                    return f"{eng}@{ap.base_partition()}_{ap.partition_size()}", 16
            except Exception:
                pass
        return eng + "@hbm", 16

    def emit(self, eng, meth, *args, r=(), w=(), **kw):
        key, inc = self._sem_key(eng, meth, args)
        is_dma = inc == 16
        if SERIAL:
            waits = {k: c for k, c in self.count.items() if c > 0 and k != key}
            self.count[key] = self.count.get(key, 0) + inc
            tok = (key, self.count[key])
            for b in [self.buf(k) for k in w]:
                b.ready = [tok]; b.users = []
            for b in [self.buf(k) for k in r]:
                b.users.append(tok)
            self.ops.append((eng, key, meth, args, kw, tuple(waits.items()), inc))
            return tok
        waits = {}
        cnt = self.count.get(key, 0)
        def add(tok, raw=False):
            k, v = tok
            if k != key:
                waits[k] = max(waits.get(k, 0), v)
            elif raw and key == "vector" and (cnt + inc - v) <= 1:
                waits[k] = max(waits.get(k, 0), v)
            elif raw and is_dma:
                waits[k] = max(waits.get(k, 0), v)
        reads = [self.buf(k) for k in r]
        writes = [self.buf(k) for k in w]
        for b in reads:
            for t in b.ready:
                add(t, raw=True)
        for b in writes:
            for t in b.ready:
                add(t)
            for t in b.users:
                add(t)
        self.count[key] = cnt + inc
        tok = (key, self.count[key])
        for b in writes:
            b.ready = [tok]
            b.users = []
        for b in reads:
            b.users.append(tok)
        self.ops.append((eng, key, meth, args, kw, tuple(waits.items()), inc))
        return tok

    def finalize(self):
        nc = self.nc
        final = [(k, c) for k, c in sorted(self.count.items()) if c > 0]
        from contextlib import ExitStack
        stack = ExitStack()
        sems = {k: stack.enter_context(
            nc.semaphore("s_" + k.replace("@", "_")))
            for k, _ in final}
        block = stack.enter_context(nc.Block())
        ops = self.ops

        def emit_for(name):
            def emitf(eng):
                done = {}
                for (enm, okey, meth, args, kw, waits, inc) in ops:
                    if enm != name:
                        continue
                    for (we, wv) in waits:
                        if done.get(we, -1) < wv:
                            eng.wait_ge(sems[we], wv)
                            done[we] = wv
                    getattr(eng, meth)(*args, **kw).then_inc(sems[okey], inc)
                for (we, wv) in final:
                    if we != name and done.get(we, -1) < wv:
                        eng.wait_ge(sems[we], wv)
            return emitf
        block.sync(emit_for("sync"))
        block.gpsimd(emit_for("gpsimd"))
        block.vector(emit_for("vector"))
        block.scalar(emit_for("scalar"))
        block.tensor(emit_for("tensor"))
        stack.close()


SERIAL = False

def build_program():
    nc = bass.Bass("TRN2", target_bir_lowering=False)
    P = Prog(nc)
    E = P.emit

    def din(name, shape, dt=F32):
        return nc.dram_tensor(name, shape, dt, kind="ExternalInput").ap()
    x_all = din("x_all", [L, D])
    w_qkv = din("w_qkv", [3 * D, D])
    w_out = din("w_out", [D, D])
    w_gate = din("w_gate", [FI, D])
    w_up = din("w_up", [FI, D])
    w_down = din("w_down", [D, FI])
    cos_f = din("cos_f", [128, L])
    sin_f = din("sin_f", [128, L])
    ident = din("ident", [128, 128], BF16)
    out = nc.dram_tensor("out", [LQ, D], F32, kind="ExternalOutput").ap()

    kscr = nc.dram_tensor("kscr", [8, 128, L], BF16, kind="Internal").ap()
    qscr = nc.dram_tensor("qscr", [H, 64, LQ], BF16, kind="Internal").ap()
    mscr = nc.dram_tensor("mscr", [H, LQ], BF16, kind="Internal").ap()
    vscr = nc.dram_tensor("vscr", [8, 32, 128, VPW], BF16, kind="Internal").ap()
    wgscr = nc.dram_tensor("wgscr", [8, 128, FI], BF16, kind="Internal").ap()
    wuscr = nc.dram_tensor("wuscr", [8, 128, FI], BF16, kind="Internal").ap()
    wdscr = nc.dram_tensor("wdscr", [FI // 128, 128, D], BF16, kind="Internal").ap()
    mdscr = nc.dram_tensor("mdscr", [H, LQ], F32, kind="Internal").ap()
    bscr = nc.dram_tensor("bscr", [16], F32, kind="Internal").ap()

    def sb(name, shape, dt):
        return nc.alloc_sbuf_tensor(name, shape, dt)

    s_id = sb("s_id", [128, 128], BF16)
    ones2 = sb("ones2", [128, 2], BF16)
    onesr = sb("onesr", [1, 128], F32)
    mq2 = sb("mq2", [1, H], F32)
    mk2 = sb("mk2", [1, H], F32)
    redt = sb("redt", [1, 1], F32)
    bsc = sb("bsc", [1, H], F32)
    bsc560 = sb("bsc560", [1, H], F32)
    mtmp = sb("mtmp", [1, 512], F32)
    mtmp2 = sb("mtmp2", [1, 512], F32)
    bscT = sb("bscT", [16, 1], F32)
    mneg = sb("mneg", [1, 512], BF16)
    ln_m = sb("ln_m", [128, 2], F32)
    ln_v = sb("ln_v", [128, 2], F32)
    ln_r = sb("ln_r", [128, 2], F32)
    wnat = [sb(f"wnat{j}", [128, D], BF16) for j in range(2)]

    base0 = nc.sbuf_base
    base = (base0 + 63) & ~63
    cursor = [base]
    placements = {}

    def place(region, name, shape, dt):
        t = nc.alloc_sbuf_tensor_at(name, shape, dt, offset=placements[region][1])
        sz = shape[-1] * mybir.dt.size(dt)
        placements[region][1] += (sz + 63) & ~63
        assert placements[region][1] <= placements[region][0] + placements[region][2], \
            f"region {region} overflow at {name}"
        return t

    def region(name, size):
        placements[name] = [cursor[0], cursor[0], size]
        cursor[0] += size

    def reset_region(name):
        placements[name][1] = placements[name][0]

    KB = 1024
    region("R1", 32 * KB)
    region("R2", 34 * KB)
    region("R3", 93 * KB)
    region("R4", 38 * KB)
    assert cursor[0] <= nc.sbuf_top - 64, f"sbuf over: {cursor[0]} > {nc.sbuf_top}"

    qfm = [place("R1", f"qfm{j}", [128, LQ], BF16) for j in range(8)]
    reset_region("R1")
    xqfm = [place("R1", f"xqfm{j}", [128, LQ], BF16) for j in range(8)]
    wTq = [place("R2", f"wTq{j}", [128, 2 * D], BF16) for j in range(8)]
    reset_region("R2")
    attn = [place("R2", f"attn{j}", [128, LQ], BF16) for j in range(8)]
    reset_region("R2")
    hbuf = place("R2", "hbuf", [128, 16 * ICH], F32)
    hqtm = place("R2", "hqtm", [128, 16 * ICH], BF16)
    hqT = [place("R2", f"hqT{j}", [128, LQ], BF16) for j in range(2)]
    silu_t = [place("R2", f"silu{j}", [128, ICH], F32) for j in range(2)]
    wT = [place("R3", f"wT{j}", [128, 3 * D], BF16) for j in range(8)]
    xs = [place("R3", f"xs{j}", [128, D], F32) for j in range(2)]
    xc = place("R3", "xc", [128, D], F32)
    sqt = place("R3", "sqt", [128, D], BF16)
    xlnt = [place("R3", f"xlnt{j}", [128, D], BF16) for j in range(2)]
    xlnf = place("R3", "xlnf", [128, 8 * 512], BF16)
    cosw = [place("R3", f"cosw{j}", [128, 512], F32) for j in range(2)]
    sinw = [place("R3", f"sinw{j}", [128, 512], F32) for j in range(2)]
    rt1 = place("R3", "rt1", [128, 512], F32)
    rt2 = place("R3", "rt2", [128, 512], F32)
    kq_pre = place("R3", "kq_pre", [128, 512], BF16)
    sq2 = place("R3", "sq2", [128, 512], BF16)
    vt = place("R3", "vt", [128, 8 * VPW], BF16)
    A_KEYS = [f"wT{j}" for j in range(8)] + \
        ["xs0", "xs1", "xc", "sqt", "xlnt0", "xlnt1", "xlnf", "cosw0", "cosw1", "sinw0", "sinw1",
         "rt1", "rt2", "kq_pre", "sq2", "vt"]
    reset_region("R3")
    kax = [place("R3", f"kA{j}", [66, L], BF16) for j in range(2)]
    kbx = [place("R3", f"kB{j}", [66, L], BF16) for j in range(2)]
    qx = [place("R3", f"qx{j}", [66, LQ], BF16) for j in range(2)]
    vpair = [place("R3", f"vpair{j}", [128, 32 * VPW], BF16) for j in range(2)]
    pt = place("R3", "pt", [128, 2048], BF16)
    rdrow = [place("R3", f"rdrow{j}", [1, 512], F32) for j in range(2)]
    rdb = [place("R3", f"rdb{j}", [64, 512], F32) for j in range(2)]
    wstr = [place("R3", f"wstr{j}", [128, 1024], F32) for j in range(2)]
    wq_stage = [place("R3", f"wq{j}", [128, FI], BF16) for j in range(2)]
    wevac = place("R3", "wevac", [128, 512], BF16)
    wevac2 = place("R3", "wevac2", [128, 512], BF16)
    B_KEYS = ["kA0", "kA1", "kB0", "kB1", "qx0", "qx1", "vpair0", "vpair1",
              "ptq0", "ptq1", "ptq2", "ptq3", "rdrow0", "rdrow1", "rdb0", "rdb1",
              "wstr0", "wstr1", "wq0", "wq1", "wevac", "wevac2"]
    reset_region("R3")
    y1 = [place("R3", f"y1_{t}", [128, D], F32) for t in range(16)]
    woutT = [place("R3", f"woutT{j}", [128, D], BF16) for j in range(8)]
    xln2 = [place("R3", f"xln2{j}", [128, D], F32) for j in range(2)]
    C_KEYS = [f"y1_{t}" for t in range(16)] + [f"woutT{j}" for j in range(8)] + \
        ["xln20", "xln21"]
    placements["R3"][1] = placements["R3"][0] + 64 * KB
    wguT = [[place("R3", f"wguT{p}_{j}", [128, 2 * ICH], BF16) for j in range(8)]
            for p in range(2)]
    wdqT = [[place("R3", f"wdqT{p}_{j}", [128, D], BF16) for j in range(2)]
            for p in range(2)]
    D_KEYS = [f"wguT{p}_{j}" for p in range(2) for j in range(8)] + \
        [f"wdqT{p}_{j}" for p in range(2) for j in range(2)]
    q_ax = place("R4", "q_ax", [128, 2048], F32)
    q_a = place("R4", "q_a", [128, 2048], F32)
    q_sg = place("R4", "q_sg", [128, 2048], F32)
    q_t0 = place("R4", "q_t0", [128, 2048], BF16)
    q_t1 = place("R4", "q_t1", [128, 2048], BF16)
    q_bm = place("R4", "q_bm", [128, 128], F32)
    q_sc = place("R4", "q_sc", [128, 128], F32)
    q_rs = place("R4", "q_rs", [128, 128], F32)
    q_out = place("R4", "q_out", [128, 2048], BF16)
    mdiag = nc.alloc_sbuf_tensor_at("mdiag", [16, 2048], F32,
                                    offset=placements["R4"][0] + 24576)
    q_abf = nc.alloc_sbuf_tensor_at("q_abf", [128, 4096], BF16,
                                    offset=placements["R4"][0] + 8192)

    P4 = nc.alloc_psum_tensor("P4", [128, 2048], F32)
    PBD1 = nc.alloc_psum_tensor("PBD1", [128, 1024], F32)
    PBD2 = nc.alloc_psum_tensor("PBD2", [128, 1024], F32)
    PB4 = PBD1.ap()[:, 0:512]
    PB5 = PBD1.ap()[:, 512:1024]
    PB6 = PBD2.ap()[:, 0:512]
    PB7 = PBD2.ap()[:, 512:1024]

    E("sync", "dma_start", s_id[:], ident, w=["id"])
    E("vector", "memset", ones2[:, :], 0.0, w=["ones2"])
    E("vector", "memset", ones2[0:64, 0:1], 1.0, r=["ones2"], w=["ones2"])
    E("vector", "memset", ones2[64:128, 1:2], 1.0, r=["ones2"], w=["ones2"])
    E("vector", "memset", onesr[:, :], 1.0, w=["onesr"])
    E("vector", "memset", mq2[:, :], 0.0, w=["mq2"])
    E("vector", "memset", mk2[:, :], 0.0, w=["mk2"])

    def transpose8(src_bf_ap, src_keys, store, bankpair=0):
        for g in range(4):
            for j in range(2):
                dc = g * 2 + j
                col = bankpair * 1024 + j * 512
                E("tensor", "matmul", P4[:, col:col + 128],
                  src_bf_ap[:, dc * 128:(dc + 1) * 128], s_id[:],
                  start=True, stop=True, r=list(src_keys) + ["id"],
                  w=[f"P4t{bankpair}_{j}"])
            for j in range(2):
                col = bankpair * 1024 + j * 512
                store(g * 2 + j, P4[:, col:col + 128], f"P4t{bankpair}_{j}")

    def build_wT(tiles, dst_col_of):
        for t in tiles:
            wn = wnat[t % 2]
            wkey = f"wnat{t % 2}"
            E("gpsimd", "dma_start", wn[:], w_qkv[t * 128:(t + 1) * 128, :], w=[wkey])
            col = dst_col_of(t)
            def st(dc, ps, pk, _c=col):
                E("vector", "tensor_copy", wT[dc][:, _c:_c + 128], ps,
                  r=[pk], w=[f"wT{dc}"])
            transpose8(wn.ap(), [wkey], st)

    def build_rot():
        for dc in range(8):
            src = wT[dc].ap()
            s3 = src[:, 0:D].rearrange("p (h t) -> p h t", t=64)
            d3 = src[:, D:2 * D].rearrange("p (h t) -> p h t", t=64)
            E("vector", "tensor_scalar", d3[:, :, 0:32], s3[:, :, 32:64], -1.0, None,
              Alu.mult, r=[f"wT{dc}"], w=[f"wT{dc}"])
            E("vector", "tensor_copy", d3[:, :, 32:64], s3[:, :, 0:32],
              r=[f"wT{dc}"], w=[f"wT{dc}"])

    def ln_tile(src_ap, src_keys, dst_ap, dst_key, par=0,
                xc_ap=None, xc_key=None, sq_ap=None, sq_key=None):
        if xc_ap is None:
            xc_ap, xc_key = xc[:], "xc"
        if sq_ap is None:
            sq_ap, sq_key = sqt[:], "sqt"
        m = ln_m[:, par:par + 1]
        v = ln_v[:, par:par + 1]
        rr = ln_r[:, par:par + 1]
        km, kv, kr = f"ln_m{par}", f"ln_v{par}", f"ln_r{par}"
        E("vector", "tensor_reduce", m, src_ap, AX.X, Alu.add, r=src_keys, w=[km])
        E("vector", "tensor_scalar", m, m, 1.0 / D, None, Alu.mult, r=[km], w=[km])
        E("vector", "tensor_scalar", xc_ap, src_ap, m, None, Alu.subtract,
          r=src_keys + [km], w=[xc_key])
        E("scalar", "activation", sq_ap, xc_ap, Act.Square, accum_out=v,
          r=[xc_key], w=[sq_key, kv])
        E("vector", "tensor_scalar", v, v, 1.0 / D, 1e-5, Alu.mult, Alu.add,
          r=[kv], w=[kv])
        E("scalar", "activation", v, v, Act.Sqrt, r=[kv], w=[kv])
        E("vector", "reciprocal", rr, v, r=[kv], w=[kr])
        E("vector", "tensor_scalar", dst_ap, xc_ap, rr, None, Alu.mult,
          r=[xc_key, kr], w=[dst_key])

    def build_xlnf(x_src, tok0_base):
        for t in range(4):
            tok0 = tok0_base + t * 128
            par = t % 2
            xsb = xs[par]
            xkey = f"xs{par}"
            E("gpsimd", "dma_start", xsb[:], x_src[tok0:tok0 + 128, :], w=[xkey])
            ln_tile(xsb[:], [xkey], xlnt[par][:], f"xlnt{par}", par=par,
                    xc_ap=q_a.ap()[:, par * 1024:(par + 1) * 1024], xc_key=f"q_a{par}",
                    sq_ap=q_sg.ap()[:, par * 1024:(par + 1) * 1024], sq_key=f"q_sg{par}")
            def stx(dc, ps, pk, _t=t):
                E("vector", "tensor_copy",
                  xlnf[:, dc * 512 + _t * 128: dc * 512 + (_t + 1) * 128], ps,
                  r=[pk], w=["xlnf"])
            transpose8(xlnt[par].ap(), [f"xlnt{par}"], stx, bankpair=par)

    def gen_rope(fc, cw_ap, sw_ap, cw_keys, out_ap, out_key, m2t, m2key,
                 wt=None, wtp="wT"):
        if wt is None:
            wt = wT
        for dc in range(8):
            E("tensor", "matmul", PB4[:, :], wt[dc][:, fc * 128:(fc + 1) * 128],
              xlnf[:, dc * 512:(dc + 1) * 512], start=(dc == 0), stop=(dc == 7),
              r=[f"{wtp}{dc}", "xlnf"], w=["PB4"])
        for dc in range(8):
            E("tensor", "matmul", PB5[:, :], wt[dc][:, D + fc * 128: D + (fc + 1) * 128],
              xlnf[:, dc * 512:(dc + 1) * 512], start=(dc == 0), stop=(dc == 7),
              r=[f"{wtp}{dc}", "xlnf"], w=["PB5f"])
        E("vector", "tensor_tensor", rt1[:], PB4[:, :], cw_ap, Alu.mult,
          r=["PB4"] + cw_keys, w=["rt1"])
        E("vector", "tensor_tensor", rt2[:], PB5[:, :], sw_ap, Alu.mult,
          r=["PB5f"] + cw_keys, w=["rt2"])
        E("vector", "tensor_tensor", out_ap, rt1[:], rt2[:], Alu.add,
          r=["rt1", "rt2"], w=[out_key])
        E("vector", "tensor_tensor", sq2[:], out_ap, out_ap, Alu.mult,
          r=[out_key], w=["sq2"])
        for par2 in range(2):
            pnb = PB6 if par2 == 0 else PB7
            pnk = "PB6" if par2 == 0 else "PB7"
            E("tensor", "matmul", pnb[0:1, :], ones2[:, par2:par2 + 1], sq2[:],
              start=True, stop=True, r=["ones2", "sq2"], w=[pnk])
        for par2 in range(2):
            hd = 2 * fc + par2
            pnb = PB6 if par2 == 0 else PB7
            pnk = "PB6" if par2 == 0 else "PB7"
            E("vector", "tensor_reduce", redt[:], pnb[0:1, :], AX.X, Alu.max,
              r=[pnk], w=["redt"])
            E("vector", "tensor_tensor", m2t[0:1, hd:hd + 1], m2t[0:1, hd:hd + 1],
              redt[:], Alu.max, r=["redt", m2key], w=[m2key])

    build_wT(range(8, 16), lambda t: (t - 8) * 128)
    build_wT(range(16, 24), lambda t: 2 * D + (t - 16) * 128)
    build_rot()
    for t in range(0, 8):
        wn = wnat[t % 2]
        wkey = f"wnat{t % 2}"
        E("gpsimd", "dma_start", wn[:], w_qkv[t * 128:(t + 1) * 128, :], w=[wkey])
        def stq_(dc, ps, pk, _c=t * 128):
            E("vector", "tensor_copy", wTq[dc][:, _c:_c + 128], ps,
              r=[pk], w=[f"wTq{dc}"])
        transpose8(wn.ap(), [wkey], stq_)
    for dc in range(8):
        srcq = wTq[dc].ap()
        s3 = srcq[:, 0:D].rearrange("p (h t) -> p h t", t=64)
        d3 = srcq[:, D:2 * D].rearrange("p (h t) -> p h t", t=64)
        E("vector", "tensor_scalar", d3[:, :, 0:32], s3[:, :, 32:64], -1.0, None,
          Alu.mult, r=[f"wTq{dc}"], w=[f"wTq{dc}"])
        E("vector", "tensor_copy", d3[:, :, 32:64], s3[:, :, 0:32],
          r=[f"wTq{dc}"], w=[f"wTq{dc}"])

    for c in range(8):
        cw0 = c * 512
        par = c % 2
        own = c < 4
        cwb, swb = cosw[par], sinw[par]
        E("sync", "dma_start", cwb[:], cos_f[:, cw0:cw0 + 512], w=[f"cosw{par}"])
        E("sync", "dma_start", swb[:], sin_f[:, cw0:cw0 + 512], w=[f"sinw{par}"])
        build_xlnf(x_all, cw0)
        for fc in range(8):
            gen_rope(fc, cwb[:], swb[:], [f"cosw{par}", f"sinw{par}"],
                     kq_pre[:], "kq_pre", mk2, "mk2")
            E("sync", "dma_start", kscr[fc, :, cw0:cw0 + 512], kq_pre[:],
              r=["kq_pre"], w=[f"kscr{fc}"])
            if own:
                gen_rope(fc, cwb[:], swb[:], [f"cosw{par}", f"sinw{par}"],
                         qfm[fc][:, cw0:cw0 + 512], f"qfm{fc}", mq2, "mq2",
                         wt=wTq, wtp="wTq")
                E("sync", "dma_start", qscr[2 * fc, :, cw0:cw0 + 512],
                  qfm[fc][0:64, cw0:cw0 + 512], r=[f"qfm{fc}"], w=["qscr"])
                E("sync", "dma_start", qscr[2 * fc + 1, :, cw0:cw0 + 512],
                  qfm[fc][64:128, cw0:cw0 + 512], r=[f"qfm{fc}"], w=["qscr"])
                E("vector", "tensor_tensor", sq2[:], qfm[fc][:, cw0:cw0 + 512],
                  kq_pre[:], Alu.mult, r=[f"qfm{fc}", "kq_pre"], w=["sq2"])
                for par2 in range(2):
                    h = 2 * fc + par2
                    pnb = PB6 if par2 == 0 else PB7
                    pnk = "PB6" if par2 == 0 else "PB7"
                    mst = mtmp if par2 == 0 else mtmp2
                    msk = "mtmp" if par2 == 0 else "mtmp2"
                    E("tensor", "matmul", pnb[0:1, :], ones2[:, par2:par2 + 1],
                      sq2[:], start=True, stop=True, r=["ones2", "sq2"], w=[pnk])
                    E("vector", "tensor_copy", mst[:], pnb[0:1, :], r=[pnk], w=[msk])
                    E("sync", "dma_start", mdscr[h, cw0:cw0 + 512], mst[:],
                      r=[msk], w=["mdscr"])
        for t in range(4):
            for half in range(2):
                pb = PB7 if half == 0 else PB4
                pk = "PB7" if half == 0 else "PB4"
                for dc in range(8):
                    E("tensor", "matmul", pb[:, :],
                      xlnf[:, dc * 512 + t * 128: dc * 512 + (t + 1) * 128],
                      wT[dc][:, 2 * D + half * 512: 2 * D + (half + 1) * 512],
                      start=(dc == 0), stop=(dc == 7),
                      r=[f"wT{dc}", "xlnf"], w=[pk])
            v4 = vt.ap().rearrange("p (h w) -> p h w", w=66)
            E("vector", "memset", v4[:, :, 64:66], 0.0, w=["vt"])
            E("vector", "memset", v4[:, :, 64:65], 1.0, r=["vt"], w=["vt"])
            for half in range(2):
                pb = PB7 if half == 0 else PB4
                pk = "PB7" if half == 0 else "PB4"
                srcv = pb.rearrange("p (hh w) -> p hh w", w=64)
                base_c = 4 * half * VPW
                dstv = vt.ap()[:, base_c: base_c + 4 * VPW].rearrange(
                    "p (hh w) -> p hh w", w=66)[:, :, 0:64]
                E("vector", "tensor_copy", dstv, srcv, r=[pk], w=["vt"])
            for pr in range(8):
                E("gpsimd", "dma_start", vscr[pr, c * 4 + t, :, :],
                  vt[:, pr * VPW:(pr + 1) * VPW], r=["vt"], w=[f"vscr{pr}"])

    E("vector", "tensor_tensor", bsc[0:1, :], mq2[0:1, :], mk2[0:1, :], Alu.mult,
      r=["mq2", "mk2"], w=["bsc"])
    E("scalar", "activation", bsc[0:1, :], bsc[0:1, :], Act.Sqrt, r=["bsc"], w=["bsc"])
    E("vector", "tensor_scalar", bsc560[0:1, :], bsc[0:1, :], 1.02, -560.0,
      Alu.mult, Alu.add, r=["bsc"], w=["bsc560"])
    E("sync", "dma_start", bscr.rearrange("(o h) -> o h", o=1), bsc560[0:1, 0:16],
      r=["bsc560"], w=["bscr"])
    E("sync", "dma_start", bscT[0:16, 0:1], bscr.rearrange("(h o) -> h o", o=1),
      r=["bscr"], w=["bscT"])
    E("sync", "dma_start", mdiag[0:16, :], mdscr[:, :], r=["mdscr"], w=["mdiag"])
    E("vector", "tensor_scalar", q_out[0:16, :], mdiag[0:16, :], bscT[0:16, 0:1],
      -1.0, Alu.max, Alu.mult, r=["mdiag", "bscT"], w=["mneg16"])
    E("sync", "dma_start", mscr[:, :], q_out[0:16, :], r=["mneg16"], w=["mscr"])

    MIDS = [0.25, 0.75, 1.25, 1.75, 2.5, 3.5, 5.0]
    STEP = [0.5, 0.5, 0.5, 0.5, 1.0, 1.0, 2.0]

    def fp4_quant(src_ap, src_keys, n, out_ap, out_key, qp=0, eng="vector"):
        assert n <= 1024 and n % 16 == 0
        b0 = qp * 1024
        h0 = qp * 64
        nb = n // 16
        ksg, kax_, ka = f"q_sg{qp}", f"q_ax{qp}", f"q_a{qp}"
        kt0, kt1 = f"q_t0{qp}", f"q_t1{qp}"
        kbm, ksc, krs, kos = f"q_bm{qp}", f"q_sc{qp}", f"q_rs{qp}", f"q_outs{qp}"
        sg = q_sg.ap()[:, b0:b0 + n]
        axp = q_ax.ap()[:, b0:b0 + n]
        ap_ = q_a.ap()[:, b0:b0 + n]
        t0 = q_t0.ap()[:, b0:b0 + n]
        t1 = q_t1.ap()[:, b0:b0 + n]
        tmp = q_out.ap()[:, b0:b0 + n]
        bm = q_bm.ap()[:, h0:h0 + nb]
        sc = q_sc.ap()[:, h0:h0 + nb]
        rs = q_rs.ap()[:, h0:h0 + nb]
        E(eng, "tensor_scalar", sg, src_ap, 0.0, 2.0, Alu.is_ge, Alu.mult,
          r=src_keys, w=[ksg])
        E(eng, "scalar_tensor_tensor", axp, sg, 1.0, src_ap,
          Alu.subtract, Alu.mult, r=[ksg] + list(src_keys), w=[kax_])
        ax3 = axp.rearrange("p (b k) -> p b k", k=16)
        E("vector", "tensor_reduce", bm, ax3, AX.X, Alu.max, r=[kax_], w=[kbm])
        E(eng, "tensor_scalar", sc, bm, 1.0 / 6.0, 1e-12,
          Alu.mult, Alu.max, r=[kbm], w=[ksc])
        E("vector", "reciprocal", rs, sc, r=[ksc], w=[krs])
        rs3 = rs.unsqueeze(2).broadcast_to([128, nb, 16])
        a3 = ap_.rearrange("p (b k) -> p b k", k=16)
        E(eng, "tensor_tensor", a3, ax3, rs3, Alu.mult, r=[kax_, krs], w=[ka])
        E(eng, "tensor_scalar", t0, ap_, MIDS[0], STEP[0],
          Alu.is_gt, Alu.mult, r=[ka], w=[kt0])
        for i in range(1, 7):
            E(eng, "tensor_scalar", t1, ap_, MIDS[i], STEP[i],
              Alu.is_gt, Alu.mult, r=[ka], w=[kt1])
            E(eng, "tensor_tensor", t0, t0, t1, Alu.add,
              r=[kt1, kt0], w=[kt0])
        E(eng, "tensor_scalar", tmp, sg, 1.0, None, Alu.subtract,
          r=[ksg], w=[kos])
        E(eng, "tensor_tensor", t1, t0, tmp, Alu.mult, r=[kt0, kos], w=[kt1])
        sc3 = sc.unsqueeze(2).broadcast_to([128, nb, 16])
        st3 = t1.rearrange("p (b k) -> p b k", k=16)
        o3 = out_ap.rearrange("p (b k) -> p b k", k=16)
        E(eng, "tensor_tensor", o3, st3, sc3, Alu.mult,
          r=[kt1, ksc], w=[out_key])

    wtiles = [("g", i) for i in range(FI // 128)] + \
             [("u", i) for i in range(FI // 128)] + \
             [("d", i) for i in range(D // 128)]
    NW = len(wtiles)
    wchunk_ctr = [0]

    def wtile_src(kind, ti):
        if kind == "g":
            return w_gate[ti * 128:(ti + 1) * 128, :], D, wgscr, "wgscr"
        if kind == "u":
            return w_up[ti * 128:(ti + 1) * 128, :], D, wuscr, "wuscr"
        return w_down[ti * 128:(ti + 1) * 128, :], FI, wdscr, "wdscr"

    def wtile_ladder(t):
        kind, ti = wtiles[t]
        src, n, scr, skey = wtile_src(kind, ti)
        wp = t % 2
        for ci, c0 in enumerate(range(0, n, 1024)):
            cn = min(1024, n - c0)
            wb = wstr[ci % 2]
            wk = f"wstr{ci % 2}"
            E("sync", "dma_start", wb[:, 0:cn], src[:, c0:c0 + cn], w=[wk])
            qp = wchunk_ctr[0] % 2
            leng = "vector"
            wchunk_ctr[0] += 1
            fp4_quant(wb.ap()[:, 0:cn], [wk], cn,
                      wq_stage[wp].ap()[:, c0:c0 + cn], f"wq{wp}", qp=qp, eng=leng)

    def wtile_store(t):
        kind, ti = wtiles[t]
        src, n, scr, skey = wtile_src(kind, ti)
        wp = t % 2
        nblk = n // 128
        for gi, lo in enumerate(range(0, nblk, 4)):
            hi = min(lo + 4, nblk)
            ng = hi - lo
            wb, wbk = ((PB5, "PB5w") if gi % 2 == 0 else (PB7, "PB7w"))
            wv, wvk = ((wevac, "wevac") if gi % 2 == 0 else (wevac2, "wevac2"))
            for k in range(lo, hi):
                E("tensor", "matmul", wb[:, (k - lo) * 128:(k - lo + 1) * 128],
                  wq_stage[wp][:, k * 128:(k + 1) * 128], s_id[:],
                  start=True, stop=True, r=[f"wq{wp}", "id"], w=[wbk])
            E("vector", "tensor_copy", wv[:, 0:ng * 128], wb[:, 0:ng * 128],
              r=[wbk], w=[wvk])
            for k in range(lo, hi):
                E("sync", "dma_start", scr[k, :, ti * 128:(ti + 1) * 128],
                  wv[:, (k - lo) * 128:(k - lo + 1) * 128],
                  r=[wvk], w=[skey])

    P.alias_seed(B_KEYS, A_KEYS)
    P.alias_seed(["P4h0", "P4h1", "P4h2", "P4h3"], ["P4t0_0", "P4t0_1"])
    P.alias_seed(["PB5w"], ["PB5f"])
    P.alias_seed(["PB7w"], ["PB7"])
    P.alias_seed([f"attn{j}" for j in range(8)], [f"wTq{j}" for j in range(8)])
    P.alias_seed(["q_t00", "q_t01", "q_t10", "q_t11"], ["mdiag"])
    P.alias_seed(["q_outs0", "q_outs1", "q_out0", "q_out1"], ["mneg16"])

    for j in range(2):
        E("vector", "memset", kax[j][64:65, :], 1.0, w=[f"kA{j}"])
        E("vector", "memset", kbx[j][64:65, :], 1.0, w=[f"kB{j}"])

    def load_pair(pr):
        vb = vpair[pr % 2]
        E("sync", "dma_start", kax[pr % 2][0:64, :], kscr[pr, 0:64, :],
          r=[f"kscr{j}" for j in range(8)], w=[f"kA{pr % 2}"])
        E("sync", "dma_start", kbx[pr % 2][0:64, :], kscr[pr, 64:128, :],
          r=[f"kscr{j}" for j in range(8)], w=[f"kB{pr % 2}"])
        for seg in range(4):
            srcv = vscr[pr, seg * 8:(seg + 1) * 8, :, :].transpose([1, 0, 2])
            dstv = vb.ap()[:, seg * 8 * VPW:(seg + 1) * 8 * VPW].rearrange(
                "p (c w) -> p c w", w=VPW)
            E("gpsimd", "dma_start", dstv, srcv,
              r=[f"vscr{pr}"], w=[f"vpair{pr % 2}"])

    def load_q(h):
        qb = qx[h % 2]
        E("sync", "dma_start", qb[0:64, :], qscr[h, :, :], r=["qscr"], w=[f"qx{h % 2}"])
        E("sync", "dma_start", qb[64:65, :], mscr[h:h + 1, :], r=["mscr"], w=[f"qx{h % 2}"])

    AVROT = [(PB4, "PB4"), (PB6, "PB6")]
    pending = []
    state = {"qci": 0, "wdone": 0}

    def emit_drain():
        pr_, hh_, qc_, avb_, avk_, di_ = pending.pop(0)
        off_ = hh_ * 64
        q0_ = qc_ * 512
        rp = di_ % 2
        E("vector", "reciprocal", rdrow[rp][:], avb_[64:65, :],
          r=[avk_], w=[f"rdrow{rp}"])
        E("tensor", "matmul", PB5[0:64, :], onesr[0:1, 0:64], rdrow[rp][:],
          start=True, stop=True, r=["onesr", f"rdrow{rp}"], w=["PB5w"])
        E("vector", "tensor_copy", rdb[rp][:], PB5[0:64, :],
          r=["PB5w"], w=[f"rdb{rp}"])
        E("vector", "tensor_tensor", attn[pr_][off_:off_ + 64, q0_:q0_ + 512],
          avb_[0:64, :], rdb[rp][:], Alu.mult,
          r=[avk_, f"rdb{rp}"], w=[f"attn{pr_}"])

    def attn_stream(pr, hh, qc, kx, kkey, qb, qkey, vb, vkey):
        qci = state["qci"]
        avb, avk = AVROT[qci % 2]
        q0 = qc * 512

        def eqk(kc):
            s = kc % 4
            E("tensor", "matmul", P4[:, s * 512:(s + 1) * 512],
              kx[0:65, kc * 128:(kc + 1) * 128], qb[0:65, q0:q0 + 512],
              start=True, stop=True, r=[kkey, qkey], w=[f"P4h{s}"])

        def eexp(k0):
            s = k0 % 4
            E("scalar", "activation", pt[:, s * 512:(s + 2) * 512],
              P4[:, s * 512:(s + 2) * 512], Act.Exp, scale=0.125,
              r=[f"P4h{s}", f"P4h{s + 1}"], w=[f"ptq{s}", f"ptq{s + 1}"])

        def eav(kc):
            s = kc % 4
            vsl = vb.ap()[:, kc * VPW + hh * 66: kc * VPW + hh * 66 + 65]
            E("tensor", "matmul", avb[0:65, :], vsl, pt[:, s * 512:(s + 1) * 512],
              start=(kc == 0), stop=(kc == 31),
              r=[vkey, f"ptq{s}"], w=[avk])

        for kc in range(32):
            eqk(kc)
            if kc % 2 == 1:
                eexp(kc - 1)
            if kc >= 3:
                eav(kc - 3)
        for kc in range(29, 32):
            eav(kc)
        pending.append((pr, hh, qc, avb, avk, qci))
        while len(pending) > 1:
            emit_drain()
        state["qci"] += 1
        wbudget = NW * state["qci"] // 64
        while state["wdone"] < wbudget:
            t = state["wdone"]
            if t >= 1:
                wtile_store(t - 1)
            wtile_ladder(t)
            state["wdone"] += 1

    load_pair(0)
    load_q(0)
    for pr in range(8):
        if pr < 7:
            load_pair(pr + 1)
        vb = vpair[pr % 2]
        vkey = f"vpair{pr % 2}"
        for hh in range(2):
            h = 2 * pr + hh
            if h < H - 1:
                load_q(h + 1)
            kx = kax[pr % 2] if hh == 0 else kbx[pr % 2]
            kkey = (f"kA{pr % 2}" if hh == 0 else f"kB{pr % 2}")
            qb = qx[h % 2]
            qkey = f"qx{h % 2}"
            for qc in range(4):
                attn_stream(pr, hh, qc, kx, kkey, qb, qkey, vb, vkey)
    while pending:
        emit_drain()
    while state["wdone"] < NW:
        t = state["wdone"]
        if t >= 1:
            wtile_store(t - 1)
        wtile_ladder(t)
        state["wdone"] += 1
    wtile_store(NW - 1)

    P.alias_seed(C_KEYS, B_KEYS)
    P.alias_seed(["PB7"], ["PB7w"])
    P.alias_seed([f"xqfm{j}" for j in range(8)], [f"qfm{j}" for j in range(8)])
    P.alias_seed(["P4t0_0", "P4t0_1"], ["P4h0", "P4h1"])
    P.alias_seed(["P4t1_0", "P4t1_1"], ["P4h2", "P4h3"])
    for t in range(8):
        wn = wnat[t % 2]
        wkey = f"wnat{t % 2}"
        E("gpsimd", "dma_start", wn[:], w_out[t * 128:(t + 1) * 128, :], w=[wkey])
        def stw(dc, ps, pk, _t=t):
            E("vector", "tensor_copy", woutT[dc][:, _t * 128:(_t + 1) * 128], ps,
              r=[pk], w=[f"woutT{dc}"])
        transpose8(wn.ap(), [wkey], stw)

    for tt in range(16):
        par = tt % 2
        E("gpsimd", "dma_start", y1[tt][:], x_all[tt * 128:(tt + 1) * 128, :],
          w=[f"y1_{tt}"])
        for half in range(2):
            pb, pk = ((PB6, "PB6") if half == 0 else (PB7, "PB7"))
            for dc in range(8):
                E("tensor", "matmul", pb[:, :], attn[dc][:, tt * 128:(tt + 1) * 128],
                  woutT[dc][:, half * 512:(half + 1) * 512],
                  start=(dc == 0), stop=(dc == 7),
                  r=[f"attn{dc}", f"woutT{dc}"], w=[pk])
        E("vector", "tensor_tensor", y1[tt][:], y1[tt][:], PBD2.ap()[:], Alu.add,
          r=["PB6", "PB7", f"y1_{tt}"], w=[f"y1_{tt}"])
        ln_tile(y1[tt][:], [f"y1_{tt}"], xln2[par][:], f"xln2{par}", par=par,
                xc_ap=q_a.ap()[:, par * 1024:(par + 1) * 1024], xc_key=f"q_a{par}",
                sq_ap=q_sg.ap()[:, par * 1024:(par + 1) * 1024], sq_key=f"q_sg{par}")
        fp4_quant(xln2[par].ap()[:], [f"xln2{par}"], D,
                  q_out.ap()[:, par * 1024:par * 1024 + D], f"q_out{par}", qp=par,
                  eng="vector")
        def stq(dc, ps, pk, _tt=tt):
            E("vector", "tensor_copy", xqfm[dc][:, _tt * 128:(_tt + 1) * 128], ps,
              r=[pk], w=[f"xqfm{dc}"])
        transpose8(q_out.ap()[:, par * 1024:par * 1024 + D], [f"q_out{par}"],
                   stq, bankpair=par)

    P.alias_seed(D_KEYS, [f"woutT{j}" for j in range(8)] + ["xln20", "xln21"])
    P.alias_seed(["hbuf0", "hbuf1", "hbuf2", "hbuf3",
                  "hqtm0", "hqtm1", "hqtm2", "hqtm3",
                  "hqT0", "hqT1", "silu0", "silu1"],
                 [f"attn{j}" for j in range(8)])
    P.alias_seed(["PB5d"], ["PB5w"])
    P.alias_seed(["PB6d"], ["PB6"])
    P.alias_seed(["PB7d"], ["PB7w"])

    for ic in range(NIC):
        par = ic % 2
        for dc in range(8):
            E("sync", "dma_start", wguT[par][dc][:, 0:ICH],
              wgscr[dc, :, ic * ICH:(ic + 1) * ICH],
              r=["wgscr"], w=[f"wguT{par}_{dc}"])
            E("gpsimd", "dma_start", wguT[par][dc][:, ICH:2 * ICH],
              wuscr[dc, :, ic * ICH:(ic + 1) * ICH],
              r=["wuscr"], w=[f"wguT{par}_{dc}"])
        for ip in range(2):
            E("sync", "dma_start", wdqT[par][ip][:], wdscr[ic * 2 + ip, :, :],
              r=["wdscr"], w=[f"wdqT{par}_{ip}"])
        for tt in range(16):
            pgu, pgk = ((PB6, "PB6d") if tt % 2 == 0 else (PB7, "PB7d"))
            for dc in range(8):
                E("tensor", "matmul", pgu[:, :], xqfm[dc][:, tt * 128:(tt + 1) * 128],
                  wguT[par][dc][:], start=(dc == 0), stop=(dc == 7),
                  r=[f"xqfm{dc}", f"wguT{par}_{dc}"], w=[pgk])
            st = silu_t[tt % 2]
            E("scalar", "activation", st[:], pgu[:, 0:ICH], Act.Silu,
              r=[pgk], w=[f"silu{tt % 2}"])
            E("vector", "tensor_tensor", hbuf[:, tt * ICH:(tt + 1) * ICH], st[:],
              pgu[:, ICH:2 * ICH], Alu.mult,
              r=[f"silu{tt % 2}", pgk], w=[f"hbuf{tt // 4}"])
            if tt % 4 == 3:
                j = tt // 4
                fp4_quant(hbuf.ap()[:, j * 1024:(j + 1) * 1024], [f"hbuf{j}"], 1024,
                          hqtm.ap()[:, j * 1024:(j + 1) * 1024], f"hqtm{j}", qp=j % 2,
                          eng="vector")
                for ip in range(2):
                    col0 = 1024 + ip * 512
                    for t2 in range(4 * j, 4 * j + 4):
                        E("tensor", "matmul", P4[:, col0 + (t2 % 4) * 128:col0 + (t2 % 4 + 1) * 128],
                          hqtm[:, t2 * ICH + ip * 128: t2 * ICH + (ip + 1) * 128],
                          s_id[:], start=True, stop=True,
                          r=[f"hqtm{j}", "id"], w=[f"P4t1_{ip}"])
                    E("vector", "tensor_copy", hqT[ip][:, 4 * j * 128:(4 * j + 4) * 128],
                      P4[:, col0:col0 + 512],
                      r=[f"P4t1_{ip}"], w=[f"hqT{ip}"])
                for t2 in range(4 * j, 4 * j + 4):
                    for half in range(2):
                        pb, pk = ((PB4, "PB4") if half == 0 else (PB5, "PB5d"))
                        for ip in range(2):
                            E("tensor", "matmul", pb[:, :],
                              hqT[ip][:, t2 * 128:(t2 + 1) * 128],
                              wdqT[par][ip][:, half * 512:(half + 1) * 512],
                              start=(ip == 0), stop=(ip == 1),
                              r=[f"hqT{ip}", f"wdqT{par}_{ip}"], w=[pk])
                    E("vector", "tensor_tensor", y1[t2][:], y1[t2][:],
                      PBD1.ap()[:], Alu.add,
                      r=["PB4", "PB5d", f"y1_{t2}"], w=[f"y1_{t2}"])

    for tt in range(16):
        E("gpsimd", "dma_start", out[tt * 128:(tt + 1) * 128, :], y1[tt][:],
          r=[f"y1_{tt}"])

    P.finalize()
    return nc


_NC = None

def _get_nc():
    global _NC
    if _NC is None:
        _NC = build_program()
    return _NC


def _tables():
    pos = np.arange(L, dtype=np.float64)
    inv = 1.0 / (10000.0 ** (np.arange(32, dtype=np.float64) / 32.0))
    invf = np.tile(inv, 4)
    ang = invf[:, None] * pos[None, :]
    return np.cos(ang).astype(np.float32), np.sin(ang).astype(np.float32)


def kernel(L0_attn_x, L0_attn_w_qkv, L0_attn_w_out, L0_ffn_w_gate, L0_ffn_w_up, L0_ffn_w_down):
    X = np.ascontiguousarray(np.asarray(L0_attn_x, dtype=np.float32))
    wqkv = np.ascontiguousarray(np.asarray(L0_attn_w_qkv, dtype=np.float32))
    wout = np.ascontiguousarray(np.asarray(L0_attn_w_out, dtype=np.float32))
    wg = np.ascontiguousarray(np.asarray(L0_ffn_w_gate, dtype=np.float32))
    wu = np.ascontiguousarray(np.asarray(L0_ffn_w_up, dtype=np.float32))
    wd = np.ascontiguousarray(np.asarray(L0_ffn_w_down, dtype=np.float32))
    cos_f, sin_f = _tables()
    ident = np.eye(128, dtype=np.float32).astype(ml_dtypes.bfloat16)

    nc = _get_nc()
    in_maps = []
    for c in range(8):
        b, hf = c // 2, c % 2
        sl = slice(hf * LQ, (hf + 1) * LQ)
        so = slice((1 - hf) * LQ, (2 - hf) * LQ)
        in_maps.append({
            "x_all": np.ascontiguousarray(
                np.concatenate([X[b, sl], X[b, so]], axis=0)),
            "w_qkv": wqkv, "w_out": wout,
            "w_gate": wg, "w_up": wu, "w_down": wd,
            "cos_f": np.ascontiguousarray(
                np.concatenate([cos_f[:, sl], cos_f[:, so]], axis=1)),
            "sin_f": np.ascontiguousarray(
                np.concatenate([sin_f[:, sl], sin_f[:, so]], axis=1)),
            "ident": ident,
        })
    res = run_bass_kernel_spmd(nc, in_maps, core_ids=list(range(8)))
    outv = np.empty((4, L, D), dtype=np.float32)
    for c in range(8):
        b, hf = c // 2, c % 2
        outv[b, hf * LQ:(hf + 1) * LQ] = res.results[c]["out"]
    return outv


# version 22 (final)
# speedup vs baseline: 1.0573x; 1.0573x over previous
import numpy as np
import ml_dtypes
import concourse.bass as bass
import concourse.mybir as mybir
from concourse.bass_utils import run_bass_kernel_spmd

F32 = mybir.dt.float32
BF16 = mybir.dt.bfloat16
Alu = mybir.AluOpType
Act = mybir.ActivationFunctionType
AX = mybir.AxisListType

D = 1024
L = 4096
LQ = 2048
H = 16
FI = 2816
ICH = 256
NIC = FI // ICH
VPW = 132


class Buf:
    __slots__ = ("ready", "users")
    def __init__(self):
        self.ready = []
        self.users = []


class Prog:
    ENGINES = ("sync", "gpsimd", "vector", "scalar", "tensor")
    DMA_ENG = {"sync", "gpsimd"}

    def __init__(self, nc):
        self.nc = nc
        self.ops = []
        self.count = {e: 0 for e in self.ENGINES}
        self.B = {}

    def buf(self, key):
        if key not in self.B:
            self.B[key] = Buf()
        return self.B[key]

    def alias_seed(self, new_keys, old_keys):
        toks = []
        for ok in old_keys:
            b = self.buf(ok)
            toks += b.ready + b.users
        for nk in new_keys:
            self.buf(nk).users.extend(toks)

    def _sem_key(self, eng, meth, args):
        if meth != "dma_start":
            return eng, 1
        for ap in args[:2]:
            try:
                if str(ap.space).endswith("SBUF"):
                    return f"{eng}@{ap.base_partition()}_{ap.partition_size()}", 16
            except Exception:
                pass
        return eng + "@hbm", 16

    def emit(self, eng, meth, *args, r=(), w=(), **kw):
        key, inc = self._sem_key(eng, meth, args)
        is_dma = inc == 16
        if SERIAL:
            waits = {k: c for k, c in self.count.items() if c > 0 and k != key}
            self.count[key] = self.count.get(key, 0) + inc
            tok = (key, self.count[key])
            for b in [self.buf(k) for k in w]:
                b.ready = [tok]; b.users = []
            for b in [self.buf(k) for k in r]:
                b.users.append(tok)
            self.ops.append((eng, key, meth, args, kw, tuple(waits.items()), inc))
            return tok
        waits = {}
        cnt = self.count.get(key, 0)
        def add(tok, raw=False):
            k, v = tok
            if k != key:
                waits[k] = max(waits.get(k, 0), v)
            elif raw and key == "vector" and (cnt + inc - v) <= 1:
                waits[k] = max(waits.get(k, 0), v)
            elif raw and is_dma:
                waits[k] = max(waits.get(k, 0), v)
        reads = [self.buf(k) for k in r]
        writes = [self.buf(k) for k in w]
        for b in reads:
            for t in b.ready:
                add(t, raw=True)
        for b in writes:
            for t in b.ready:
                add(t)
            for t in b.users:
                add(t)
        self.count[key] = cnt + inc
        tok = (key, self.count[key])
        for b in writes:
            b.ready = [tok]
            b.users = []
        for b in reads:
            b.users.append(tok)
        self.ops.append((eng, key, meth, args, kw, tuple(waits.items()), inc))
        return tok

    def finalize(self):
        nc = self.nc
        final = [(k, c) for k, c in sorted(self.count.items()) if c > 0]
        from contextlib import ExitStack
        stack = ExitStack()
        sems = {k: stack.enter_context(
            nc.semaphore("s_" + k.replace("@", "_")))
            for k, _ in final}
        block = stack.enter_context(nc.Block())
        ops = self.ops

        def emit_for(name):
            def emitf(eng):
                done = {}
                for (enm, okey, meth, args, kw, waits, inc) in ops:
                    if enm != name:
                        continue
                    for (we, wv) in waits:
                        if done.get(we, -1) < wv:
                            eng.wait_ge(sems[we], wv)
                            done[we] = wv
                    getattr(eng, meth)(*args, **kw).then_inc(sems[okey], inc)
                for (we, wv) in final:
                    if we != name and done.get(we, -1) < wv:
                        eng.wait_ge(sems[we], wv)
            return emitf
        block.sync(emit_for("sync"))
        block.gpsimd(emit_for("gpsimd"))
        block.vector(emit_for("vector"))
        block.scalar(emit_for("scalar"))
        block.tensor(emit_for("tensor"))
        stack.close()


SERIAL = False

def build_program():
    nc = bass.Bass("TRN2", target_bir_lowering=False)
    P = Prog(nc)
    E = P.emit

    def din(name, shape, dt=F32):
        return nc.dram_tensor(name, shape, dt, kind="ExternalInput").ap()
    x_all = din("x_all", [L, D])
    w_qkv = din("w_qkv", [3 * D, D])
    w_out = din("w_out", [D, D])
    w_gate = din("w_gate", [FI, D])
    w_up = din("w_up", [FI, D])
    w_down = din("w_down", [D, FI])
    cos_f = din("cos_f", [128, L])
    sin_f = din("sin_f", [128, L])
    ident = din("ident", [128, 128], BF16)
    out = nc.dram_tensor("out", [LQ, D], F32, kind="ExternalOutput").ap()

    kscr = nc.dram_tensor("kscr", [8, 128, L], BF16, kind="Internal").ap()
    qscr = nc.dram_tensor("qscr", [H, 64, LQ], BF16, kind="Internal").ap()
    mscr = nc.dram_tensor("mscr", [H, LQ], BF16, kind="Internal").ap()
    vscr = nc.dram_tensor("vscr", [8, 32, 128, VPW], BF16, kind="Internal").ap()
    wgscr = nc.dram_tensor("wgscr", [8, 128, FI], BF16, kind="Internal").ap()
    wuscr = nc.dram_tensor("wuscr", [8, 128, FI], BF16, kind="Internal").ap()
    wdscr = nc.dram_tensor("wdscr", [FI // 128, 128, D], BF16, kind="Internal").ap()
    mdscr = nc.dram_tensor("mdscr", [H, LQ], F32, kind="Internal").ap()
    bscr = nc.dram_tensor("bscr", [16], F32, kind="Internal").ap()

    def sb(name, shape, dt):
        return nc.alloc_sbuf_tensor(name, shape, dt)

    s_id = sb("s_id", [128, 128], BF16)
    ones2 = sb("ones2", [128, 2], BF16)
    onesr = sb("onesr", [1, 128], F32)
    mq2 = sb("mq2", [1, H], F32)
    mk2 = sb("mk2", [1, H], F32)
    redt = sb("redt", [1, 1], F32)
    bsc = sb("bsc", [1, H], F32)
    bsc560 = sb("bsc560", [1, H], F32)
    mtmp = sb("mtmp", [1, 512], F32)
    mtmp2 = sb("mtmp2", [1, 512], F32)
    bscT = sb("bscT", [16, 1], F32)
    mneg = sb("mneg", [1, 512], BF16)
    ln_m = sb("ln_m", [128, 2], F32)
    ln_v = sb("ln_v", [128, 2], F32)
    ln_r = sb("ln_r", [128, 2], F32)
    wnat = [sb(f"wnat{j}", [128, D], BF16) for j in range(2)]

    base0 = nc.sbuf_base
    base = (base0 + 63) & ~63
    cursor = [base]
    placements = {}

    def place(region, name, shape, dt):
        t = nc.alloc_sbuf_tensor_at(name, shape, dt, offset=placements[region][1])
        sz = shape[-1] * mybir.dt.size(dt)
        placements[region][1] += (sz + 63) & ~63
        assert placements[region][1] <= placements[region][0] + placements[region][2], \
            f"region {region} overflow at {name}"
        return t

    def region(name, size):
        placements[name] = [cursor[0], cursor[0], size]
        cursor[0] += size

    def reset_region(name):
        placements[name][1] = placements[name][0]

    KB = 1024
    region("R1", 32 * KB)
    region("R2", 34 * KB)
    region("R3", 93 * KB)
    region("R4", 38 * KB)
    assert cursor[0] <= nc.sbuf_top - 64, f"sbuf over: {cursor[0]} > {nc.sbuf_top}"

    qfm = [place("R1", f"qfm{j}", [128, LQ], BF16) for j in range(8)]
    reset_region("R1")
    xqfm = [place("R1", f"xqfm{j}", [128, LQ], BF16) for j in range(8)]
    wTq = [place("R2", f"wTq{j}", [128, 2 * D], BF16) for j in range(8)]
    reset_region("R2")
    attn = [place("R2", f"attn{j}", [128, LQ], BF16) for j in range(8)]
    reset_region("R2")
    hbuf = place("R2", "hbuf", [128, 16 * ICH], F32)
    hqtm = place("R2", "hqtm", [128, 16 * ICH], BF16)
    hqT = [place("R2", f"hqT{j}", [128, LQ], BF16) for j in range(2)]
    silu_t = [place("R2", f"silu{j}", [128, ICH], F32) for j in range(2)]
    wT = [place("R3", f"wT{j}", [128, 3 * D], BF16) for j in range(8)]
    xs = [place("R3", f"xs{j}", [128, D], F32) for j in range(2)]
    xc = place("R3", "xc", [128, D], F32)
    sqt = place("R3", "sqt", [128, D], BF16)
    xlnt = [place("R3", f"xlnt{j}", [128, D], BF16) for j in range(2)]
    xlnf = place("R3", "xlnf", [128, 8 * 512], BF16)
    cosw = [place("R3", f"cosw{j}", [128, 512], F32) for j in range(2)]
    sinw = [place("R3", f"sinw{j}", [128, 512], F32) for j in range(2)]
    rt1 = place("R3", "rt1", [128, 512], F32)
    rt2 = place("R3", "rt2", [128, 512], F32)
    kq_pre = place("R3", "kq_pre", [128, 512], BF16)
    sq2 = place("R3", "sq2", [128, 512], BF16)
    vt = place("R3", "vt", [128, 8 * VPW], BF16)
    A_KEYS = [f"wT{j}" for j in range(8)] + \
        ["xs0", "xs1", "xc", "sqt", "xlnt0", "xlnt1", "xlnf", "cosw0", "cosw1", "sinw0", "sinw1",
         "rt1", "rt2", "kq_pre", "sq2", "vt"]
    reset_region("R3")
    kax = [place("R3", f"kA{j}", [66, L], BF16) for j in range(2)]
    kbx = [place("R3", f"kB{j}", [66, L], BF16) for j in range(2)]
    qx = [place("R3", f"qx{j}", [66, LQ], BF16) for j in range(2)]
    vpair = [place("R3", f"vpair{j}", [128, 32 * VPW], BF16) for j in range(2)]
    pt = place("R3", "pt", [128, 2048], BF16)
    rdrow = [place("R3", f"rdrow{j}", [1, 512], F32) for j in range(2)]
    rdb = [place("R3", f"rdb{j}", [64, 512], F32) for j in range(2)]
    wstr = [place("R3", f"wstr{j}", [128, 1024], F32) for j in range(2)]
    wq_stage = [place("R3", f"wq{j}", [128, FI], BF16) for j in range(2)]
    wevac = place("R3", "wevac", [128, 512], BF16)
    wevac2 = place("R3", "wevac2", [128, 512], BF16)
    B_KEYS = ["kA0", "kA1", "kB0", "kB1", "qx0", "qx1", "vpair0", "vpair1",
              "ptq0", "ptq1", "ptq2", "ptq3", "rdrow0", "rdrow1", "rdb0", "rdb1",
              "wstr0", "wstr1", "wq0", "wq1", "wevac", "wevac2"]
    reset_region("R3")
    y1 = [place("R3", f"y1_{t}", [128, D], F32) for t in range(16)]
    woutT = [place("R3", f"woutT{j}", [128, D], BF16) for j in range(8)]
    xln2 = [place("R3", f"xln2{j}", [128, D], F32) for j in range(2)]
    C_KEYS = [f"y1_{t}" for t in range(16)] + [f"woutT{j}" for j in range(8)] + \
        ["xln20", "xln21"]
    placements["R3"][1] = placements["R3"][0] + 64 * KB
    wguT = [[place("R3", f"wguT{p}_{j}", [128, 2 * ICH], BF16) for j in range(8)]
            for p in range(2)]
    wdqT = [[place("R3", f"wdqT{p}_{j}", [128, D], BF16) for j in range(2)]
            for p in range(2)]
    D_KEYS = [f"wguT{p}_{j}" for p in range(2) for j in range(8)] + \
        [f"wdqT{p}_{j}" for p in range(2) for j in range(2)]
    q_ax = place("R4", "q_ax", [128, 2048], F32)
    q_a = place("R4", "q_a", [128, 2048], F32)
    q_sg = place("R4", "q_sg", [128, 2048], F32)
    q_t0 = place("R4", "q_t0", [128, 2048], BF16)
    q_t1 = place("R4", "q_t1", [128, 2048], BF16)
    q_bm = place("R4", "q_bm", [128, 128], F32)
    q_sc = place("R4", "q_sc", [128, 128], F32)
    q_rs = place("R4", "q_rs", [128, 128], F32)
    q_out = place("R4", "q_out", [128, 2048], BF16)
    mdiag = nc.alloc_sbuf_tensor_at("mdiag", [16, 2048], F32,
                                    offset=placements["R4"][0] + 24576)

    P4 = nc.alloc_psum_tensor("P4", [128, 2048], F32)
    PB4 = nc.alloc_psum_tensor("PB4", [128, 512], F32)
    PB5 = nc.alloc_psum_tensor("PB5", [128, 512], F32)
    PB6 = nc.alloc_psum_tensor("PB6", [128, 512], F32)
    PB7 = nc.alloc_psum_tensor("PB7", [128, 512], F32)

    E("sync", "dma_start", s_id[:], ident, w=["id"])
    E("vector", "memset", ones2[:, :], 0.0, w=["ones2"])
    E("vector", "memset", ones2[0:64, 0:1], 1.0, r=["ones2"], w=["ones2"])
    E("vector", "memset", ones2[64:128, 1:2], 1.0, r=["ones2"], w=["ones2"])
    E("vector", "memset", onesr[:, :], 1.0, w=["onesr"])
    E("vector", "memset", mq2[:, :], 0.0, w=["mq2"])
    E("vector", "memset", mk2[:, :], 0.0, w=["mk2"])

    def transpose8(src_bf_ap, src_keys, store, bankpair=0):
        for g in range(4):
            for j in range(2):
                dc = g * 2 + j
                col = bankpair * 1024 + j * 512
                E("tensor", "matmul", P4[:, col:col + 128],
                  src_bf_ap[:, dc * 128:(dc + 1) * 128], s_id[:],
                  start=True, stop=True, r=list(src_keys) + ["id"],
                  w=[f"P4t{bankpair}_{j}"])
            for j in range(2):
                col = bankpair * 1024 + j * 512
                store(g * 2 + j, P4[:, col:col + 128], f"P4t{bankpair}_{j}")

    def build_wT(tiles, dst_col_of):
        for t in tiles:
            wn = wnat[t % 2]
            wkey = f"wnat{t % 2}"
            E("gpsimd", "dma_start", wn[:], w_qkv[t * 128:(t + 1) * 128, :], w=[wkey])
            col = dst_col_of(t)
            def st(dc, ps, pk, _c=col):
                E("vector", "tensor_copy", wT[dc][:, _c:_c + 128], ps,
                  r=[pk], w=[f"wT{dc}"])
            transpose8(wn.ap(), [wkey], st)

    def build_rot():
        for dc in range(8):
            src = wT[dc].ap()
            s3 = src[:, 0:D].rearrange("p (h t) -> p h t", t=64)
            d3 = src[:, D:2 * D].rearrange("p (h t) -> p h t", t=64)
            E("vector", "tensor_scalar", d3[:, :, 0:32], s3[:, :, 32:64], -1.0, None,
              Alu.mult, r=[f"wT{dc}"], w=[f"wT{dc}"])
            E("vector", "tensor_copy", d3[:, :, 32:64], s3[:, :, 0:32],
              r=[f"wT{dc}"], w=[f"wT{dc}"])

    def ln_tile(src_ap, src_keys, dst_ap, dst_key, par=0,
                xc_ap=None, xc_key=None, sq_ap=None, sq_key=None):
        if xc_ap is None:
            xc_ap, xc_key = xc[:], "xc"
        if sq_ap is None:
            sq_ap, sq_key = sqt[:], "sqt"
        m = ln_m[:, par:par + 1]
        v = ln_v[:, par:par + 1]
        rr = ln_r[:, par:par + 1]
        km, kv, kr = f"ln_m{par}", f"ln_v{par}", f"ln_r{par}"
        E("vector", "tensor_reduce", m, src_ap, AX.X, Alu.add, r=src_keys, w=[km])
        E("vector", "tensor_scalar", m, m, 1.0 / D, None, Alu.mult, r=[km], w=[km])
        E("vector", "tensor_scalar", xc_ap, src_ap, m, None, Alu.subtract,
          r=src_keys + [km], w=[xc_key])
        E("scalar", "activation", sq_ap, xc_ap, Act.Square, accum_out=v,
          r=[xc_key], w=[sq_key, kv])
        E("vector", "tensor_scalar", v, v, 1.0 / D, 1e-5, Alu.mult, Alu.add,
          r=[kv], w=[kv])
        E("scalar", "activation", v, v, Act.Sqrt, r=[kv], w=[kv])
        E("vector", "reciprocal", rr, v, r=[kv], w=[kr])
        E("vector", "tensor_scalar", dst_ap, xc_ap, rr, None, Alu.mult,
          r=[xc_key, kr], w=[dst_key])

    def build_xlnf(x_src, tok0_base):
        for t in range(4):
            tok0 = tok0_base + t * 128
            par = t % 2
            xsb = xs[par]
            xkey = f"xs{par}"
            E("gpsimd", "dma_start", xsb[:], x_src[tok0:tok0 + 128, :], w=[xkey])
            ln_tile(xsb[:], [xkey], xlnt[par][:], f"xlnt{par}", par=par,
                    xc_ap=q_a.ap()[:, par * 1024:(par + 1) * 1024], xc_key=f"q_a{par}",
                    sq_ap=q_sg.ap()[:, par * 1024:(par + 1) * 1024], sq_key=f"q_sg{par}")
            def stx(dc, ps, pk, _t=t):
                E("vector", "tensor_copy",
                  xlnf[:, dc * 512 + _t * 128: dc * 512 + (_t + 1) * 128], ps,
                  r=[pk], w=["xlnf"])
            transpose8(xlnt[par].ap(), [f"xlnt{par}"], stx, bankpair=par)

    def gen_rope(fc, cw_ap, sw_ap, cw_keys, out_ap, out_key, m2t, m2key,
                 wt=None, wtp="wT"):
        if wt is None:
            wt = wT
        for dc in range(8):
            E("tensor", "matmul", PB4[:, :], wt[dc][:, fc * 128:(fc + 1) * 128],
              xlnf[:, dc * 512:(dc + 1) * 512], start=(dc == 0), stop=(dc == 7),
              r=[f"{wtp}{dc}", "xlnf"], w=["PB4"])
        for dc in range(8):
            E("tensor", "matmul", PB5[:, :], wt[dc][:, D + fc * 128: D + (fc + 1) * 128],
              xlnf[:, dc * 512:(dc + 1) * 512], start=(dc == 0), stop=(dc == 7),
              r=[f"{wtp}{dc}", "xlnf"], w=["PB5f"])
        E("vector", "tensor_tensor", rt1[:], PB4[:, :], cw_ap, Alu.mult,
          r=["PB4"] + cw_keys, w=["rt1"])
        E("vector", "tensor_tensor", rt2[:], PB5[:, :], sw_ap, Alu.mult,
          r=["PB5f"] + cw_keys, w=["rt2"])
        E("vector", "tensor_tensor", out_ap, rt1[:], rt2[:], Alu.add,
          r=["rt1", "rt2"], w=[out_key])
        E("vector", "tensor_tensor", sq2[:], out_ap, out_ap, Alu.mult,
          r=[out_key], w=["sq2"])
        for par2 in range(2):
            pnb = PB6 if par2 == 0 else PB7
            pnk = "PB6" if par2 == 0 else "PB7"
            E("tensor", "matmul", pnb[0:1, :], ones2[:, par2:par2 + 1], sq2[:],
              start=True, stop=True, r=["ones2", "sq2"], w=[pnk])
        for par2 in range(2):
            hd = 2 * fc + par2
            pnb = PB6 if par2 == 0 else PB7
            pnk = "PB6" if par2 == 0 else "PB7"
            E("vector", "tensor_reduce", redt[:], pnb[0:1, :], AX.X, Alu.max,
              r=[pnk], w=["redt"])
            E("vector", "tensor_tensor", m2t[0:1, hd:hd + 1], m2t[0:1, hd:hd + 1],
              redt[:], Alu.max, r=["redt", m2key], w=[m2key])

    build_wT(range(8, 16), lambda t: (t - 8) * 128)
    build_wT(range(16, 24), lambda t: 2 * D + (t - 16) * 128)
    build_rot()
    for t in range(0, 8):
        wn = wnat[t % 2]
        wkey = f"wnat{t % 2}"
        E("gpsimd", "dma_start", wn[:], w_qkv[t * 128:(t + 1) * 128, :], w=[wkey])
        def stq_(dc, ps, pk, _c=t * 128):
            E("vector", "tensor_copy", wTq[dc][:, _c:_c + 128], ps,
              r=[pk], w=[f"wTq{dc}"])
        transpose8(wn.ap(), [wkey], stq_)
    for dc in range(8):
        srcq = wTq[dc].ap()
        s3 = srcq[:, 0:D].rearrange("p (h t) -> p h t", t=64)
        d3 = srcq[:, D:2 * D].rearrange("p (h t) -> p h t", t=64)
        E("vector", "tensor_scalar", d3[:, :, 0:32], s3[:, :, 32:64], -1.0, None,
          Alu.mult, r=[f"wTq{dc}"], w=[f"wTq{dc}"])
        E("vector", "tensor_copy", d3[:, :, 32:64], s3[:, :, 0:32],
          r=[f"wTq{dc}"], w=[f"wTq{dc}"])

    v4i = vt.ap().rearrange("p (h w) -> p h w", w=66)
    E("vector", "memset", v4i[:, :, 64:66], 0.0, w=["vt"])
    E("vector", "memset", v4i[:, :, 64:65], 1.0, r=["vt"], w=["vt"])
    for c in range(8):
        cw0 = c * 512
        par = c % 2
        own = c < 4
        cwb, swb = cosw[par], sinw[par]
        E("sync", "dma_start", cwb[:], cos_f[:, cw0:cw0 + 512], w=[f"cosw{par}"])
        E("sync", "dma_start", swb[:], sin_f[:, cw0:cw0 + 512], w=[f"sinw{par}"])
        build_xlnf(x_all, cw0)
        for fc in range(8):
            gen_rope(fc, cwb[:], swb[:], [f"cosw{par}", f"sinw{par}"],
                     kq_pre[:], "kq_pre", mk2, "mk2")
            E("sync", "dma_start", kscr[fc, :, cw0:cw0 + 512], kq_pre[:],
              r=["kq_pre"], w=[f"kscr{fc}"])
            if own:
                gen_rope(fc, cwb[:], swb[:], [f"cosw{par}", f"sinw{par}"],
                         qfm[fc][:, cw0:cw0 + 512], f"qfm{fc}", mq2, "mq2",
                         wt=wTq, wtp="wTq")
                E("sync", "dma_start", qscr[2 * fc, :, cw0:cw0 + 512],
                  qfm[fc][0:64, cw0:cw0 + 512], r=[f"qfm{fc}"], w=["qscr"])
                E("sync", "dma_start", qscr[2 * fc + 1, :, cw0:cw0 + 512],
                  qfm[fc][64:128, cw0:cw0 + 512], r=[f"qfm{fc}"], w=["qscr"])
                E("vector", "tensor_tensor", sq2[:], qfm[fc][:, cw0:cw0 + 512],
                  kq_pre[:], Alu.mult, r=[f"qfm{fc}", "kq_pre"], w=["sq2"])
                for par2 in range(2):
                    h = 2 * fc + par2
                    pnb = PB6 if par2 == 0 else PB7
                    pnk = "PB6" if par2 == 0 else "PB7"
                    mst = mtmp if par2 == 0 else mtmp2
                    msk = "mtmp" if par2 == 0 else "mtmp2"
                    E("tensor", "matmul", pnb[0:1, :], ones2[:, par2:par2 + 1],
                      sq2[:], start=True, stop=True, r=["ones2", "sq2"], w=[pnk])
                    E("vector", "tensor_copy", mst[:], pnb[0:1, :], r=[pnk], w=[msk])
                    E("sync", "dma_start", mdscr[h, cw0:cw0 + 512], mst[:],
                      r=[msk], w=["mdscr"])
        for t in range(4):
            for half in range(2):
                pb = PB7 if half == 0 else PB4
                pk = "PB7" if half == 0 else "PB4"
                for dc in range(8):
                    E("tensor", "matmul", pb[:, :],
                      xlnf[:, dc * 512 + t * 128: dc * 512 + (t + 1) * 128],
                      wT[dc][:, 2 * D + half * 512: 2 * D + (half + 1) * 512],
                      start=(dc == 0), stop=(dc == 7),
                      r=[f"wT{dc}", "xlnf"], w=[pk])
            for half in range(2):
                pb = PB7 if half == 0 else PB4
                pk = "PB7" if half == 0 else "PB4"
                srcv = pb.ap().rearrange("p (hh w) -> p hh w", w=64)
                base_c = 4 * half * VPW
                dstv = vt.ap()[:, base_c: base_c + 4 * VPW].rearrange(
                    "p (hh w) -> p hh w", w=66)[:, :, 0:64]
                E("vector", "tensor_copy", dstv, srcv, r=[pk], w=["vt"])
            for pr in range(8):
                E("gpsimd", "dma_start", vscr[pr, c * 4 + t, :, :],
                  vt[:, pr * VPW:(pr + 1) * VPW], r=["vt"], w=[f"vscr{pr}"])

    E("vector", "tensor_tensor", bsc[0:1, :], mq2[0:1, :], mk2[0:1, :], Alu.mult,
      r=["mq2", "mk2"], w=["bsc"])
    E("scalar", "activation", bsc[0:1, :], bsc[0:1, :], Act.Sqrt, r=["bsc"], w=["bsc"])
    E("vector", "tensor_scalar", bsc560[0:1, :], bsc[0:1, :], 1.02, -560.0,
      Alu.mult, Alu.add, r=["bsc"], w=["bsc560"])
    E("sync", "dma_start", bscr.rearrange("(o h) -> o h", o=1), bsc560[0:1, 0:16],
      r=["bsc560"], w=["bscr"])
    E("sync", "dma_start", bscT[0:16, 0:1], bscr.rearrange("(h o) -> h o", o=1),
      r=["bscr"], w=["bscT"])
    E("sync", "dma_start", mdiag[0:16, :], mdscr[:, :], r=["mdscr"], w=["mdiag"])
    E("vector", "tensor_scalar", q_out[0:16, :], mdiag[0:16, :], bscT[0:16, 0:1],
      -1.0, Alu.max, Alu.mult, r=["mdiag", "bscT"], w=["mneg16"])
    E("sync", "dma_start", mscr[:, :], q_out[0:16, :], r=["mneg16"], w=["mscr"])

    MIDS = [0.25, 0.75, 1.25, 1.75, 2.5, 3.5, 5.0]
    STEP = [0.5, 0.5, 0.5, 0.5, 1.0, 1.0, 2.0]

    def fp4_quant(src_ap, src_keys, n, out_ap, out_key, qp=0, eng="vector"):
        assert n <= 1024 and n % 16 == 0
        b0 = qp * 1024
        h0 = qp * 64
        nb = n // 16
        ksg, kax_, ka = f"q_sg{qp}", f"q_ax{qp}", f"q_a{qp}"
        kt0, kt1 = f"q_t0{qp}", f"q_t1{qp}"
        kbm, ksc, krs, kos = f"q_bm{qp}", f"q_sc{qp}", f"q_rs{qp}", f"q_outs{qp}"
        sg = q_sg.ap()[:, b0:b0 + n]
        axp = q_ax.ap()[:, b0:b0 + n]
        ap_ = q_a.ap()[:, b0:b0 + n]
        t0 = q_t0.ap()[:, b0:b0 + n]
        t1 = q_t1.ap()[:, b0:b0 + n]
        tmp = q_out.ap()[:, b0:b0 + n]
        bm = q_bm.ap()[:, h0:h0 + nb]
        sc = q_sc.ap()[:, h0:h0 + nb]
        rs = q_rs.ap()[:, h0:h0 + nb]
        E(eng, "tensor_scalar", sg, src_ap, 0.0, 2.0, Alu.is_ge, Alu.mult,
          r=src_keys, w=[ksg])
        E(eng, "scalar_tensor_tensor", axp, sg, 1.0, src_ap,
          Alu.subtract, Alu.mult, r=[ksg] + list(src_keys), w=[kax_])
        ax3 = axp.rearrange("p (b k) -> p b k", k=16)
        E("vector", "tensor_reduce", bm, ax3, AX.X, Alu.max, r=[kax_], w=[kbm])
        E(eng, "tensor_scalar", sc, bm, 1.0 / 6.0, 1e-12,
          Alu.mult, Alu.max, r=[kbm], w=[ksc])
        E("vector", "reciprocal", rs, sc, r=[ksc], w=[krs])
        rs3 = rs.unsqueeze(2).broadcast_to([128, nb, 16])
        a3 = ap_.rearrange("p (b k) -> p b k", k=16)
        E(eng, "tensor_tensor", a3, ax3, rs3, Alu.mult, r=[kax_, krs], w=[ka])
        E(eng, "tensor_scalar", t0, ap_, MIDS[0], STEP[0],
          Alu.is_gt, Alu.mult, r=[ka], w=[kt0])
        for i in range(1, 7):
            E(eng, "tensor_scalar", t1, ap_, MIDS[i], STEP[i],
              Alu.is_gt, Alu.mult, r=[ka], w=[kt1])
            E(eng, "tensor_tensor", t0, t0, t1, Alu.add,
              r=[kt1, kt0], w=[kt0])
        E(eng, "tensor_scalar", tmp, sg, 1.0, None, Alu.subtract,
          r=[ksg], w=[kos])
        E(eng, "tensor_tensor", t1, t0, tmp, Alu.mult, r=[kt0, kos], w=[kt1])
        sc3 = sc.unsqueeze(2).broadcast_to([128, nb, 16])
        st3 = t1.rearrange("p (b k) -> p b k", k=16)
        o3 = out_ap.rearrange("p (b k) -> p b k", k=16)
        E(eng, "tensor_tensor", o3, st3, sc3, Alu.mult,
          r=[kt1, ksc], w=[out_key])

    wtiles = [("g", i) for i in range(FI // 128)] + \
             [("u", i) for i in range(FI // 128)] + \
             [("d", i) for i in range(D // 128)]
    NW = len(wtiles)
    wchunk_ctr = [0]

    def wtile_src(kind, ti):
        if kind == "g":
            return w_gate[ti * 128:(ti + 1) * 128, :], D, wgscr, "wgscr"
        if kind == "u":
            return w_up[ti * 128:(ti + 1) * 128, :], D, wuscr, "wuscr"
        return w_down[ti * 128:(ti + 1) * 128, :], FI, wdscr, "wdscr"

    def wtile_ladder(t):
        kind, ti = wtiles[t]
        src, n, scr, skey = wtile_src(kind, ti)
        wp = t % 2
        for ci, c0 in enumerate(range(0, n, 1024)):
            cn = min(1024, n - c0)
            wb = wstr[ci % 2]
            wk = f"wstr{ci % 2}"
            E("sync", "dma_start", wb[:, 0:cn], src[:, c0:c0 + cn], w=[wk])
            qp = wchunk_ctr[0] % 2
            leng = "vector"
            wchunk_ctr[0] += 1
            fp4_quant(wb.ap()[:, 0:cn], [wk], cn,
                      wq_stage[wp].ap()[:, c0:c0 + cn], f"wq{wp}", qp=qp, eng=leng)

    def wtile_store(t):
        kind, ti = wtiles[t]
        src, n, scr, skey = wtile_src(kind, ti)
        wp = t % 2
        nblk = n // 128
        for gi, lo in enumerate(range(0, nblk, 4)):
            hi = min(lo + 4, nblk)
            ng = hi - lo
            wb, wbk = ((PB5, "PB5w") if gi % 2 == 0 else (PB7, "PB7w"))
            wv, wvk = ((wevac, "wevac") if gi % 2 == 0 else (wevac2, "wevac2"))
            for k in range(lo, hi):
                E("tensor", "matmul", wb[:, (k - lo) * 128:(k - lo + 1) * 128],
                  wq_stage[wp][:, k * 128:(k + 1) * 128], s_id[:],
                  start=True, stop=True, r=[f"wq{wp}", "id"], w=[wbk])
            E("vector", "tensor_copy", wv[:, 0:ng * 128], wb[:, 0:ng * 128],
              r=[wbk], w=[wvk])
            for k in range(lo, hi):
                E("sync", "dma_start", scr[k, :, ti * 128:(ti + 1) * 128],
                  wv[:, (k - lo) * 128:(k - lo + 1) * 128],
                  r=[wvk], w=[skey])

    P.alias_seed(B_KEYS, A_KEYS)
    P.alias_seed(["P4h0", "P4h1", "P4h2", "P4h3"], ["P4t0_0", "P4t0_1"])
    P.alias_seed(["PB5w"], ["PB5f"])
    P.alias_seed(["PB7w"], ["PB7"])
    P.alias_seed([f"attn{j}" for j in range(8)], [f"wTq{j}" for j in range(8)])
    P.alias_seed(["q_t00", "q_t01", "q_t10", "q_t11"], ["mdiag"])
    P.alias_seed(["q_outs0", "q_outs1", "q_out0", "q_out1"], ["mneg16"])

    for j in range(2):
        E("vector", "memset", kax[j][64:65, :], 1.0, w=[f"kA{j}"])
        E("vector", "memset", kbx[j][64:65, :], 1.0, w=[f"kB{j}"])

    def load_pair(pr):
        vb = vpair[pr % 2]
        E("sync", "dma_start", kax[pr % 2][0:64, :], kscr[pr, 0:64, :],
          r=[f"kscr{j}" for j in range(8)], w=[f"kA{pr % 2}"])
        E("sync", "dma_start", kbx[pr % 2][0:64, :], kscr[pr, 64:128, :],
          r=[f"kscr{j}" for j in range(8)], w=[f"kB{pr % 2}"])
        for seg in range(4):
            srcv = vscr[pr, seg * 8:(seg + 1) * 8, :, :].transpose([1, 0, 2])
            dstv = vb.ap()[:, seg * 8 * VPW:(seg + 1) * 8 * VPW].rearrange(
                "p (c w) -> p c w", w=VPW)
            E("gpsimd", "dma_start", dstv, srcv,
              r=[f"vscr{pr}"], w=[f"vpair{pr % 2}"])

    def load_q(h):
        qb = qx[h % 2]
        E("sync", "dma_start", qb[0:64, :], qscr[h, :, :], r=["qscr"], w=[f"qx{h % 2}"])
        E("sync", "dma_start", qb[64:65, :], mscr[h:h + 1, :], r=["mscr"], w=[f"qx{h % 2}"])

    AVROT = [(PB4, "PB4"), (PB6, "PB6")]
    pending = []
    state = {"qci": 0, "wdone": 0}

    def emit_drain():
        pr_, hh_, qc_, avb_, avk_, di_ = pending.pop(0)
        off_ = hh_ * 64
        q0_ = qc_ * 512
        rp = di_ % 2
        E("tensor", "matmul", PB5[0:64, :], onesr[0:1, 0:64], rdrow[rp][:],
          start=True, stop=True, r=["onesr", f"rdrow{rp}"], w=["PB5w"])
        E("vector", "tensor_copy", rdb[rp][:], PB5[0:64, :],
          r=["PB5w"], w=[f"rdb{rp}"])
        E("vector", "tensor_tensor", attn[pr_][off_:off_ + 64, q0_:q0_ + 512],
          avb_[0:64, :], rdb[rp][:], Alu.mult,
          r=[avk_, f"rdb{rp}"], w=[f"attn{pr_}"])

    def attn_stream(pr, hh, qc, kx, kkey, qb, qkey, vb, vkey):
        qci = state["qci"]
        avb, avk = AVROT[qci % 2]
        q0 = qc * 512

        def eqk(kc):
            s = kc % 4
            E("tensor", "matmul", P4[:, s * 512:(s + 1) * 512],
              kx[0:65, kc * 128:(kc + 1) * 128], qb[0:65, q0:q0 + 512],
              start=True, stop=True, r=[kkey, qkey], w=[f"P4h{s}"])

        def eexp(k0):
            s = k0 % 4
            E("scalar", "activation", pt[:, s * 512:(s + 2) * 512],
              P4[:, s * 512:(s + 2) * 512], Act.Exp, scale=0.125,
              r=[f"P4h{s}", f"P4h{s + 1}"], w=[f"ptq{s}", f"ptq{s + 1}"])

        def eav(kc):
            s = kc % 4
            vsl = vb.ap()[:, kc * VPW + hh * 66: kc * VPW + hh * 66 + 65]
            E("tensor", "matmul", avb[0:65, :], vsl, pt[:, s * 512:(s + 1) * 512],
              start=(kc == 0), stop=(kc == 31),
              r=[vkey, f"ptq{s}"], w=[avk])

        for kc in range(32):
            eqk(kc)
            if kc % 2 == 1:
                eexp(kc - 1)
            if kc >= 3:
                eav(kc - 3)
        for kc in range(29, 32):
            eav(kc)
        E("vector", "reciprocal", rdrow[qci % 2][:], avb[64:65, :],
          r=[avk], w=[f"rdrow{qci % 2}"])
        pending.append((pr, hh, qc, avb, avk, qci))
        while len(pending) > 1:
            emit_drain()
        state["qci"] += 1
        wbudget = NW * state["qci"] // 64
        while state["wdone"] < wbudget:
            t = state["wdone"]
            if t >= 1:
                wtile_store(t - 1)
            wtile_ladder(t)
            state["wdone"] += 1

    load_pair(0)
    load_q(0)
    for pr in range(8):
        if pr < 7:
            load_pair(pr + 1)
        vb = vpair[pr % 2]
        vkey = f"vpair{pr % 2}"
        for hh in range(2):
            h = 2 * pr + hh
            if h < H - 1:
                load_q(h + 1)
            kx = kax[pr % 2] if hh == 0 else kbx[pr % 2]
            kkey = (f"kA{pr % 2}" if hh == 0 else f"kB{pr % 2}")
            qb = qx[h % 2]
            qkey = f"qx{h % 2}"
            for qc in range(4):
                attn_stream(pr, hh, qc, kx, kkey, qb, qkey, vb, vkey)
    while pending:
        emit_drain()
    while state["wdone"] < NW:
        t = state["wdone"]
        if t >= 1:
            wtile_store(t - 1)
        wtile_ladder(t)
        state["wdone"] += 1
    wtile_store(NW - 1)

    P.alias_seed(C_KEYS, B_KEYS)
    P.alias_seed(["PB7"], ["PB7w"])
    P.alias_seed([f"xqfm{j}" for j in range(8)], [f"qfm{j}" for j in range(8)])
    P.alias_seed(["P4t0_0", "P4t0_1"], ["P4h0", "P4h1"])
    P.alias_seed(["P4t1_0", "P4t1_1"], ["P4h2", "P4h3"])
    for t in range(8):
        wn = wnat[t % 2]
        wkey = f"wnat{t % 2}"
        E("gpsimd", "dma_start", wn[:], w_out[t * 128:(t + 1) * 128, :], w=[wkey])
        def stw(dc, ps, pk, _t=t):
            E("vector", "tensor_copy", woutT[dc][:, _t * 128:(_t + 1) * 128], ps,
              r=[pk], w=[f"woutT{dc}"])
        transpose8(wn.ap(), [wkey], stw)

    for tt in range(16):
        par = tt % 2
        E("gpsimd", "dma_start", y1[tt][:], x_all[tt * 128:(tt + 1) * 128, :],
          w=[f"y1_{tt}"])
        for half in range(2):
            pb, pk = ((PB6, "PB6") if half == 0 else (PB7, "PB7"))
            for dc in range(8):
                E("tensor", "matmul", pb[:, :], attn[dc][:, tt * 128:(tt + 1) * 128],
                  woutT[dc][:, half * 512:(half + 1) * 512],
                  start=(dc == 0), stop=(dc == 7),
                  r=[f"attn{dc}", f"woutT{dc}"], w=[pk])
            E("vector", "tensor_tensor", y1[tt][:, half * 512:(half + 1) * 512],
              y1[tt][:, half * 512:(half + 1) * 512], pb[:, :], Alu.add,
              r=[pk, f"y1_{tt}"], w=[f"y1_{tt}"])
        ln_tile(y1[tt][:], [f"y1_{tt}"], xln2[par][:], f"xln2{par}", par=par,
                xc_ap=q_a.ap()[:, par * 1024:(par + 1) * 1024], xc_key=f"q_a{par}",
                sq_ap=q_sg.ap()[:, par * 1024:(par + 1) * 1024], sq_key=f"q_sg{par}")
        fp4_quant(xln2[par].ap()[:], [f"xln2{par}"], D,
                  q_out.ap()[:, par * 1024:par * 1024 + D], f"q_out{par}", qp=par,
                  eng="vector")
        def stq(dc, ps, pk, _tt=tt):
            E("vector", "tensor_copy", xqfm[dc][:, _tt * 128:(_tt + 1) * 128], ps,
              r=[pk], w=[f"xqfm{dc}"])
        transpose8(q_out.ap()[:, par * 1024:par * 1024 + D], [f"q_out{par}"],
                   stq, bankpair=par)

    P.alias_seed(D_KEYS, [f"woutT{j}" for j in range(8)] + ["xln20", "xln21"])
    P.alias_seed(["hbuf0", "hbuf1", "hbuf2", "hbuf3",
                  "hqtm0", "hqtm1", "hqtm2", "hqtm3",
                  "hqT0", "hqT1", "silu0", "silu1"],
                 [f"attn{j}" for j in range(8)])
    P.alias_seed(["PB5d"], ["PB5w"])
    P.alias_seed(["PB6d"], ["PB6"])
    P.alias_seed(["PB7d"], ["PB7w"])

    for ic in range(NIC):
        par = ic % 2
        for dc in range(8):
            E("sync", "dma_start", wguT[par][dc][:, 0:ICH],
              wgscr[dc, :, ic * ICH:(ic + 1) * ICH],
              r=["wgscr"], w=[f"wguT{par}_{dc}"])
            E("gpsimd", "dma_start", wguT[par][dc][:, ICH:2 * ICH],
              wuscr[dc, :, ic * ICH:(ic + 1) * ICH],
              r=["wuscr"], w=[f"wguT{par}_{dc}"])
        for ip in range(2):
            E("sync", "dma_start", wdqT[par][ip][:], wdscr[ic * 2 + ip, :, :],
              r=["wdscr"], w=[f"wdqT{par}_{ip}"])
        for tt in range(16):
            pgu, pgk = ((PB6, "PB6d") if tt % 2 == 0 else (PB7, "PB7d"))
            for dc in range(8):
                E("tensor", "matmul", pgu[:, :], xqfm[dc][:, tt * 128:(tt + 1) * 128],
                  wguT[par][dc][:], start=(dc == 0), stop=(dc == 7),
                  r=[f"xqfm{dc}", f"wguT{par}_{dc}"], w=[pgk])
            st = silu_t[tt % 2]
            E("scalar", "activation", st[:], pgu[:, 0:ICH], Act.Silu,
              r=[pgk], w=[f"silu{tt % 2}"])
            E("vector", "tensor_tensor", hbuf[:, tt * ICH:(tt + 1) * ICH], st[:],
              pgu[:, ICH:2 * ICH], Alu.mult,
              r=[f"silu{tt % 2}", pgk], w=[f"hbuf{tt // 4}"])
            if tt % 4 == 3:
                j = tt // 4
                fp4_quant(hbuf.ap()[:, j * 1024:(j + 1) * 1024], [f"hbuf{j}"], 1024,
                          hqtm.ap()[:, j * 1024:(j + 1) * 1024], f"hqtm{j}", qp=j % 2,
                          eng="vector")
                for ip in range(2):
                    col0 = 1024 + ip * 512
                    for t2 in range(4 * j, 4 * j + 4):
                        E("tensor", "matmul", P4[:, col0 + (t2 % 4) * 128:col0 + (t2 % 4 + 1) * 128],
                          hqtm[:, t2 * ICH + ip * 128: t2 * ICH + (ip + 1) * 128],
                          s_id[:], start=True, stop=True,
                          r=[f"hqtm{j}", "id"], w=[f"P4t1_{ip}"])
                    E("vector", "tensor_copy", hqT[ip][:, 4 * j * 128:(4 * j + 4) * 128],
                      P4[:, col0:col0 + 512],
                      r=[f"P4t1_{ip}"], w=[f"hqT{ip}"])
                for t2 in range(4 * j, 4 * j + 4):
                    for half in range(2):
                        pb, pk = ((PB4, "PB4") if half == 0 else (PB5, "PB5d"))
                        for ip in range(2):
                            E("tensor", "matmul", pb[:, :],
                              hqT[ip][:, t2 * 128:(t2 + 1) * 128],
                              wdqT[par][ip][:, half * 512:(half + 1) * 512],
                              start=(ip == 0), stop=(ip == 1),
                              r=[f"hqT{ip}", f"wdqT{par}_{ip}"], w=[pk])
                        E("vector", "tensor_tensor",
                          y1[t2][:, half * 512:(half + 1) * 512],
                          y1[t2][:, half * 512:(half + 1) * 512], pb[:, :], Alu.add,
                          r=[pk, f"y1_{t2}"], w=[f"y1_{t2}"])

    for tt in range(16):
        E("gpsimd", "dma_start", out[tt * 128:(tt + 1) * 128, :], y1[tt][:],
          r=[f"y1_{tt}"])

    P.finalize()
    return nc


_NC = None

def _get_nc():
    global _NC
    if _NC is None:
        _NC = build_program()
    return _NC


def _tables():
    pos = np.arange(L, dtype=np.float64)
    inv = 1.0 / (10000.0 ** (np.arange(32, dtype=np.float64) / 32.0))
    invf = np.tile(inv, 4)
    ang = invf[:, None] * pos[None, :]
    return np.cos(ang).astype(np.float32), np.sin(ang).astype(np.float32)


def kernel(L0_attn_x, L0_attn_w_qkv, L0_attn_w_out, L0_ffn_w_gate, L0_ffn_w_up, L0_ffn_w_down):
    X = np.ascontiguousarray(np.asarray(L0_attn_x, dtype=np.float32))
    wqkv = np.ascontiguousarray(np.asarray(L0_attn_w_qkv, dtype=np.float32))
    wout = np.ascontiguousarray(np.asarray(L0_attn_w_out, dtype=np.float32))
    wg = np.ascontiguousarray(np.asarray(L0_ffn_w_gate, dtype=np.float32))
    wu = np.ascontiguousarray(np.asarray(L0_ffn_w_up, dtype=np.float32))
    wd = np.ascontiguousarray(np.asarray(L0_ffn_w_down, dtype=np.float32))
    cos_f, sin_f = _tables()
    ident = np.eye(128, dtype=np.float32).astype(ml_dtypes.bfloat16)

    nc = _get_nc()
    in_maps = []
    for c in range(8):
        b, hf = c // 2, c % 2
        sl = slice(hf * LQ, (hf + 1) * LQ)
        so = slice((1 - hf) * LQ, (2 - hf) * LQ)
        in_maps.append({
            "x_all": np.ascontiguousarray(
                np.concatenate([X[b, sl], X[b, so]], axis=0)),
            "w_qkv": wqkv, "w_out": wout,
            "w_gate": wg, "w_up": wu, "w_down": wd,
            "cos_f": np.ascontiguousarray(
                np.concatenate([cos_f[:, sl], cos_f[:, so]], axis=1)),
            "sin_f": np.ascontiguousarray(
                np.concatenate([sin_f[:, sl], sin_f[:, so]], axis=1)),
            "ident": ident,
        })
    res = run_bass_kernel_spmd(nc, in_maps, core_ids=list(range(8)))
    outv = np.empty((4, L, D), dtype=np.float32)
    for c in range(8):
        b, hf = c // 2, c % 2
        outv[b, hf * LQ:(hf + 1) * LQ] = res.results[c]["out"]
    return outv
